# version 20
# baseline (speedup 1.0000x reference)
import numpy as np
import ml_dtypes
from contextlib import ExitStack

import concourse.bass as bass
import concourse.mybir as mybir
from concourse.bass_utils import run_bass_kernel_spmd

F32 = mybir.dt.float32
BF16 = mybir.dt.bfloat16
AF = mybir.ActivationFunctionType
ALU = mybir.AluOpType
AX = mybir.AxisListType

T = 2048
NTT = 16
NTP = 17
D = 1024
ND = 8
FF = 4096
NFT = 32
NE = 8
EPS = 1e-5
ENGINES = ("sync", "scalar", "vector", "tensor", "gpsimd")
SELF_SYNC = {"vector", "scalar", "gpsimd"}


class _Sem:
    __slots__ = ("h", "n", "waited")

    def __init__(self, h):
        self.h = h
        self.n = 0
        self.waited = {}


class _KB:
    def __init__(self, nc, esem):
        self.nc = nc
        self.q = {e: [] for e in ENGINES}
        self.esem = esem

    def op(self, eng, fn, inc=None, by=1):
        if eng in SELF_SYNC:
            assert inc is None
            es = self.esem[eng]
            self.wt(eng, es, es.n)
            inc, by = es, 1
        if inc is not None:
            inc.n += by
        self.q[eng].append((fn, inc, by))
        return inc.n if inc is not None else None

    def dma(self, eng, fn, inc):
        inc.n += 16
        self.q[eng].append((fn, inc, 16))
        return inc.n

    def wt(self, eng, sem, val):
        if val <= 0:
            return
        assert val <= sem.n, f"wait {eng} for {val} but planned only {sem.n}"
        self._wt(eng, sem, val)

    def wt_future(self, eng, sem, val):
        if val <= 0:
            return
        self._wt(eng, sem, val)

    def _wt(self, eng, sem, val):
        if val <= sem.waited.get(eng, 0):
            return
        sem.waited[eng] = val
        nc = self.nc

        def f(e=eng, s=sem, v=val):
            return getattr(nc, e).wait_ge(s.h, v)
        self.q[eng].append((f, None, 0))

    def run(self, eng):
        for fn, inc, by in self.q[eng]:
            r = fn()
            if inc is not None:
                r.then_inc(inc.h, by)


def _build(stage="full", n_experts=NE):
    nc = bass.Bass("TRN2", target_bir_lowering=False)

    def din(name, shape):
        return nc.dram_tensor(name, shape, F32, kind="ExternalInput").ap()

    U = din("U", [2176, 1024])
    W1L = din("W1L", [NE, 16, 128, 2048])
    W2L = din("W2L", [NE, 2, 8, 128, 2048])
    RW1L = din("RW1L", [4, 128, 2048])
    WBTL = din("WBTL", [4, 128, 2048])
    WBFTL = din("WBFTL", [4, 128, 2048])
    RW2L = din("RW2L", [128, 64])
    B1L = din("B1L", [128, 256])
    B2L = din("B2L", [8, 1024])
    RB1L = din("RB1L", [128, 8])
    RB2R = din("RB2R", [128, 8])
    W1R = din("W1R", [128, 1024])
    W2R = din("W2R", [128, 1024])
    BREP = din("BREP", [128, 1])
    M0 = din("M0", [128, 16])
    ML = din("ML", [128, 16])
    IDENT = din("IDENT", [128, 128])
    WBL = din("WBL", [8, 128, 1024])
    WBFL = din("WBFL", [8, 128, 1024])
    W1C = din("W1C", [128, 8])
    W2C = din("W2C", [128, 8])
    OUT = nc.dram_tensor("OUT", [128, 16384], F32, kind="ExternalOutput").ap()

    with ExitStack() as stack:
        e = stack.enter_context
        xnT = [e(nc.sbuf_tensor(f"xnT{i}", [128, 2052], BF16)) for i in range(ND)]
        hidA = [e(nc.sbuf_tensor(f"hid{i}", [128, 1024], BF16)) for i in range(32)]
        g = e(nc.sbuf_tensor("g", [128, 16384], F32))
        wstage = [e(nc.sbuf_tensor(f"wst{i}", [128, 2048], F32)) for i in range(2)]
        wbf = [e(nc.sbuf_tensor(f"wbf{i}", [128, 2048], BF16)) for i in range(2)]
        ustage = [e(nc.sbuf_tensor(f"ust{i}", [128, 1024], F32)) for i in range(2)]
        ostage = e(nc.sbuf_tensor("ost", [128, 1024], F32))
        b1sb = e(nc.sbuf_tensor("b1sb", [128, 256], F32))
        rb1 = e(nc.sbuf_tensor("rb1", [128, 8], F32))
        rb2r = e(nc.sbuf_tensor("rb2r", [128, 8], F32))
        rw2sb = e(nc.sbuf_tensor("rw2sb", [128, 64], F32))
        rw2bf = e(nc.sbuf_tensor("rw2bf", [128, 64], BF16))
        b2bf = e(nc.sbuf_tensor("b2bf", [8, 1024], BF16))
        selT = e(nc.sbuf_tensor("selT", [8, 2048], BF16))
        selm = e(nc.sbuf_tensor("selm", [128, 128], F32))
        m0 = e(nc.sbuf_tensor("m0", [128, 16], F32))
        mL = e(nc.sbuf_tensor("mL", [128, 16], F32))
        ident = e(nc.sbuf_tensor("ident", [128, 128], F32))
        brep = e(nc.sbuf_tensor("brep", [128, 1], F32))
        w2rb = e(nc.sbuf_tensor("w2rb", [128, 1024], BF16))
        w1cb = e(nc.sbuf_tensor("w1cb", [128, 8], BF16))
        w2cb = e(nc.sbuf_tensor("w2cb", [128, 8], BF16))
        w12c = e(nc.sbuf_tensor("w12c", [128, 16], F32))
        wvs = e(nc.sbuf_tensor("wvs", [128, 16], BF16))
        mhalf = e(nc.sbuf_tensor("mhalf", [128, 1], F32))
        cs = e(nc.sbuf_tensor("cs", [128, 32], F32))
        lgt = e(nc.sbuf_tensor("lgt", [128, 8], F32))
        pr = e(nc.sbuf_tensor("pr", [128, 8], F32))
        eqm = e(nc.sbuf_tensor("eqm", [128, 8], F32))
        PS = e(nc.psum_tensor("PS", [128, 4096], F32))

        SV = _Sem(e(nc.semaphore("SV")))
        SS = _Sem(e(nc.semaphore("SS")))
        SG = _Sem(e(nc.semaphore("SG")))
        S_sc = _Sem(e(nc.semaphore("S_sc")))
        S_wd = [_Sem(e(nc.semaphore(f"S_wd{i}"))) for i in range(4)]
        S_u = [_Sem(e(nc.semaphore(f"S_u{i}"))) for i in range(4)]
        S_out = _Sem(e(nc.semaphore("S_out")))
        S_tp = _Sem(e(nc.semaphore("S_tp")))
        S_mm = _Sem(e(nc.semaphore("S_mm")))
        S_wuse = _Sem(e(nc.semaphore("S_wuse")))

        esem = {"vector": SV, "scalar": SS, "gpsimd": SG}
        kb = _KB(nc, esem)
        scrA = wstage[0].ap()[:, 0:1024]
        scrB = wstage[0].ap()[:, 1024:2048]
        scrC = wstage[1].ap()[:, 0:1024]
        w1r = wstage[1].ap()[:, 1024:2048]
        junkF = wbf[1].ap().bitcast(F32)[:, 0:1024]
        jnk = hidA[0].ap().bitcast(F32)
        hc = g.ap().bitcast(BF16)

        def hview(dd, half):
            return hc[:, (2 * dd + half) * 1024:(2 * dd + half) * 1024 + 1024]

        for dst, src in [(ident, IDENT), (m0, M0), (mL, ML), (b1sb, B1L),
                         (rb1, RB1L), (rb2r, RB2R), (rw2sb, RW2L)]:
            kb.dma("scalar", lambda d=dst, s=src: nc.scalar.dma_start(out=d[:], in_=s), S_sc)
        kb.dma("scalar", lambda: nc.scalar.dma_start(out=ustage[0][0:8, :], in_=B2L), S_sc)
        kb.dma("scalar", lambda: nc.scalar.dma_start(out=w12c[:, 0:8], in_=W1C), S_sc)
        kb.dma("scalar", lambda: nc.scalar.dma_start(out=w12c[:, 8:16], in_=W2C), S_sc)
        n_small = 10
        kb.wt("vector", S_sc, 16 * n_small)
        b2_done = kb.op("vector", lambda: nc.vector.tensor_copy(b2bf[:], ustage[0][0:8, :]))
        kb.op("vector", lambda: nc.vector.tensor_copy(w1cb[:], w12c[:, 0:8]))
        w12_done = kb.op("vector", lambda: nc.vector.tensor_copy(w2cb[:], w12c[:, 8:16]))
        pre_done = kb.op("vector", lambda: nc.vector.tensor_copy(rw2bf[:], rw2sb[:]))
        kb.op("gpsimd", lambda: nc.gpsimd.memset(mhalf[:], -0.5))

        def ln_acc(tt, usrc, cb):
            for half, col in ((0, cb + 0), (1, cb + 1)):
                kb.op("scalar", lambda h=half, c=col, u=usrc: nc.scalar.activation(
                    jnk[:], u[:, h * 512:(h + 1) * 512], AF.Identity,
                    accum_out=cs[:, c:c + 1]))
            for half, col in ((0, cb + 2), (1, cb + 3)):
                acc = kb.op("scalar", lambda h=half, c=col, u=usrc: nc.scalar.activation(
                    jnk[:], u[:, h * 512:(h + 1) * 512], AF.Square,
                    accum_out=cs[:, c:c + 1]))
            return acc

        def ln_vec(tt, usrc, xn_out, acc, cb):
            kb.wt("vector", SS, acc)
            kb.op("vector", lambda c=cb: nc.vector.tensor_tensor(
                cs[:, c + 4:c + 5], cs[:, c:c + 1], cs[:, c + 1:c + 2], op=ALU.add))
            kb.op("vector", lambda c=cb: nc.vector.tensor_scalar(
                cs[:, c + 4:c + 5], cs[:, c + 4:c + 5], 1.0 / D, None, op0=ALU.mult))
            kb.op("vector", lambda c=cb: nc.vector.tensor_tensor(
                cs[:, c + 5:c + 6], cs[:, c + 4:c + 5], cs[:, c + 4:c + 5], op=ALU.mult))
            kb.op("vector", lambda c=cb: nc.vector.tensor_tensor(
                cs[:, c + 6:c + 7], cs[:, c + 2:c + 3], cs[:, c + 3:c + 4], op=ALU.add))
            kb.op("vector", lambda c=cb: nc.vector.tensor_scalar(
                cs[:, c + 6:c + 7], cs[:, c + 6:c + 7], 1.0 / D, cs[:, c + 5:c + 6],
                op0=ALU.mult, op1=ALU.subtract))
            vpe = kb.op("vector", lambda c=cb: nc.vector.tensor_scalar(
                cs[:, c + 6:c + 7], cs[:, c + 6:c + 7], EPS, None, op0=ALU.add))
            kb.wt("gpsimd", SV, vpe)
            pw = kb.op("gpsimd", lambda c=cb: nc.gpsimd.tensor_tensor(
                cs[:, c + 7:c + 8], cs[:, c + 6:c + 7], mhalf[:], op=ALU.pow))
            kb.wt("vector", SG, pw)
            return kb.op("vector", lambda c=cb, u=usrc, o=xn_out: nc.vector.tensor_scalar(
                o, u[:], cs[:, c + 4:c + 5], cs[:, c + 7:c + 8],
                op0=ALU.subtract, op1=ALU.mult))

        plan = []
        pidx = {}

        def padd(key, src, eng):
            pidx[key] = len(plan)
            plan.append((src, eng, True, None))

        def padd_R(tcq):
            for c in range(4):
                for h in range(2):
                    padd(("R", tcq, c, h), RW1L[c, :, h * 1024:(h + 1) * 1024],
                         "vector" if h == 0 else "scalar")

        def padd_A(ex, blk, c):
            for h in range(2):
                padd(("A", ex, blk, c, h), W1L[ex, c, :, h * 1024:(h + 1) * 1024],
                     "vector" if h == 0 else "scalar")

        A00_PRO = (2, 3, 4, 5) if n_experts else ()
        padd_R(0)
        padd_R(1)
        for _c in A00_PRO[:2]:
            padd_A(0, 0, _c)
        padd_R(2)
        for _c in A00_PRO[2:]:
            padd_A(0, 0, _c)
        padd_R(3)
        for _w, _src in ((0, WBL), (1, WBFL)):
            for dt in range(ND):
                padd(("wv", _w, dt), _src[dt], "vector" if dt % 2 == 0 else "scalar")
        for _ex in range(n_experts):
            for _blk in range(2):
                for c in range(16):
                    if (_ex, _blk) == (0, 0) and c in A00_PRO:
                        continue
                    padd_A(_ex, _blk, c)
                for dh in range(2):
                    for c in range(8):
                        for h in range(2):
                            padd(("B", _ex, _blk, dh, c, h),
                                 W2L[_ex, dh, c, :, h * 1024:(h + 1) * 1024],
                                 "vector" if h == 0 else "scalar")
        n_premm = len(plan)
        for c in range(4):
            for h in range(2):
                plan.append((WBTL[c, :, h * 1024:(h + 1) * 1024],
                             ("vector" if h == 0 else "scalar"), False, 16 + 2 * c + h))
        for c in range(4):
            for h in range(2):
                plan.append((WBFTL[c, :, h * 1024:(h + 1) * 1024],
                             ("vector" if h == 0 else "scalar"), False, 24 + 2 * c + h))
        wuse_after = []
        acc_w = 0
        for p in plan:
            acc_w += 1 if p[2] else 0
            wuse_after.append(acc_w)
        total_wuse = acc_w
        cast_done = []
        emitted = [0]
        whf = [wstage[i][:, o * 1024:(o + 1) * 1024] for i in range(2) for o in range(2)]
        whb = [wbf[i][:, o * 1024:(o + 1) * 1024] for i in range(2) for o in range(2)]

        def ensure(upto):
            while emitted[0] < min(upto, len(plan)):
                j = emitted[0]
                src, eng, has_mm, wdst = plan[j]
                slot = j % 4
                if j >= 4:
                    sem, val = cast_done[j - 4]
                    kb.wt("sync", sem, val)
                kb.dma("sync", lambda s=src, i=slot: nc.sync.dma_start(
                    out=whf[i], in_=s), S_wd[slot])
                kb.wt(eng, S_wd[slot], 16 * (j // 4 + 1))
                if j >= 4:
                    kb.wt_future(eng, S_wuse, wuse_after[j - 4])
                dst = whb[slot] if wdst is None else hidA[wdst][:]
                if wdst is not None:
                    kb.wt_future(eng, S_wuse, wuse_after[n_premm - 1])
                if eng == "vector":
                    kb.op(eng, lambda d=dst, i=slot: nc.vector.tensor_copy(d, whf[i]))
                else:
                    kb.op(eng, lambda d=dst, i=slot: nc.scalar.copy(d, whf[i]))
                cast_done.append((esem[eng], esem[eng].n))
                emitted[0] += 1

        def use_chunk(j):
            ensure(j + 5)
            kb.wt("tensor", cast_done[j][0], cast_done[j][1])
            return j

        wvrow = ostage.ap()[0:1, 0:512]
        wv_game_done = [0, 0]

        def emit_wv_which(which, xe_gate):
            wc = w1cb if which == 0 else w2cb
            kb.wt("tensor", SV, w12_done)
            kb.wt("tensor", SS, max(bankR_free))
            for dt in range(ND):
                gh = use_chunk(pidx[("wv", which, dt)])
                for nch in range(2):
                    bank = which * 2 + nch
                    kb.op("tensor", lambda d=dt, b=bank, n_=nch, i=gh % 4, w=wc:
                          nc.tensor.matmul(PS[0:1, b * 512:(b + 1) * 512],
                                           w[:, d:d + 1],
                                           whb[i][:, n_ * 512:(n_ + 1) * 512],
                                           start=(d == 0), stop=(d == ND - 1)),
                          inc=(S_wuse if nch == 1 else None))
            kb.wt("vector", S_wuse, wuse_after[pidx[("wv", which, ND - 1)]])
            kb.wt("vector", S_tp, tp_done[NTP - 1])
            kb.wt("tensor", SV, xe_gate)
            for nch in range(2):
                bank = which * 2 + nch
                kb.op("vector", lambda b=bank: nc.vector.tensor_copy(
                    wvrow[:, :], PS[0:1, b * 512:(b + 1) * 512]))
                rowcp = SV.n
                for k in range(4):
                    kb.wt("tensor", SV, rowcp)
                    kb.op("tensor", lambda K=k: nc.tensor.transpose(
                        PS[:, 2048 + K: 2048 + K + 1],
                        wvrow[0:1, K * 128:(K + 1) * 128], ident[0:1, 0:1]),
                        inc=(S_tp if k == 3 else None))
                kb.wt("vector", S_tp, S_tp.n)
                for k in range(4):
                    kb.op("vector", lambda K=k, w=which, n_=nch: nc.vector.tensor_copy(
                        wvs[:, w * 8 + n_ * 4 + K: w * 8 + n_ * 4 + K + 1],
                        PS[:, 2048 + K: 2048 + K + 1]))
            wv_game_done[which] = SV.n

        bankR_free = [0, 0, 0, 0]
        grpq = [0]

        def emit_router_tcq(tcq):
            kb.wt("scalar", S_sc, 16 * n_small)
            kb.wt("tensor", SV, max(wv_game_done))
            for c in range(4):
                for sub in range(2):
                    gh = use_chunk(pidx[("R", tcq, c, sub)])
                    kb.wt("tensor", SV, xe_done[tcq * 4 + 3])
                    dd = 2 * c + sub
                    bank = grpq[0] % 4
                    grpq[0] += 1
                    kb.wt("tensor", SS, bankR_free[bank])
                    for dt in range(ND):
                        kb.op("tensor", lambda d=dt, b=bank, q=tcq, i=gh % 4:
                              nc.tensor.matmul(PS[:, b * 512:(b + 1) * 512],
                                               whb[i][:, d * 128:(d + 1) * 128],
                                               xnT[d][:, q * 512:(q + 1) * 512],
                                               start=(d == 0), stop=(d == ND - 1)),
                              inc=(S_wuse if dt == ND - 1 else None))
                    kb.wt("scalar", S_wuse, wuse_after[gh])
                    kb.op("scalar", lambda d=dd, q=tcq, b=bank: nc.scalar.activation(
                        hview(d, q // 2)[:, (q % 2) * 512:(q % 2) * 512 + 512],
                        PS[:, b * 512:(b + 1) * 512], AF.Relu, bias=rb1[:, d:d + 1]))
                    bankR_free[bank] = SS.n

        def emit_A_chunk(ex, blk, c, bankA, ent_Be):
            kb.wt("tensor", SV, xe_done[7 if blk == 0 else NTT - 1])
            if (ex, blk, c) == (0, 0, A00_PRO[0]) and n_experts:
                kb.wt("scalar", SV, wv_game_done[1])
            for sub in range(2):
                gh = use_chunk(pidx[("A", ex, blk, c, sub)])
                fp = 2 * c + sub
                for tc in range(2):
                    bank = (fp % 2) * 2 + tc
                    if ent_Be:
                        kb.wt("tensor", SV, be_bank[bank])
                    kb.wt("tensor", SS, bankA[bank])
                    for dt in range(ND):
                        chunk_last = dt == ND - 1 and tc == 1
                        kb.op("tensor", lambda d=dt, b=bank, q=tc, i=gh % 4, bl=blk:
                              nc.tensor.matmul(PS[:, b * 512:(b + 1) * 512],
                                               whb[i][:, d * 128:(d + 1) * 128],
                                               xnT[d][:, bl * 1024 + q * 512: bl * 1024 + q * 512 + 512],
                                               start=(d == 0), stop=(d == ND - 1)),
                              inc=(S_wuse if chunk_last else None))
                kb.wt("scalar", S_wuse, wuse_after[gh])
                pair = (fp % 2) * 2
                kb.op("scalar", lambda f=fp, p=pair, E=ex: nc.scalar.activation(
                    hidA[f][:], PS[:, p * 512: p * 512 + 1024], AF.Relu,
                    bias=b1sb[:, E * 32 + f: E * 32 + f + 1]))
                bankA[pair] = bankA[pair + 1] = SS.n

        uslot = [ustage[0].ap()[:, :], ustage[1].ap()[:, :],
                 g.ap()[:, 8192:9216], g.ap()[:, 9216:10240]]
        xn_done = {}
        tp_done = {}
        xe_done = {}
        acc_pro = {}
        u_ms = {}
        kb.wt("scalar", SV, pre_done)
        for t in range(4):
            u_ms[t] = kb.dma("scalar", lambda t_=t: nc.scalar.dma_start(
                out=uslot[t_], in_=U[t_ * 128:(t_ + 1) * 128, :]), S_u[t])
        for tt in range(NTP):
            if tt == 0:
                kb.wt("tensor", S_sc, 16 * n_small)
            kb.wt("scalar", S_u[tt % 4], u_ms[tt])
            acc_pro[tt] = ln_acc(tt, uslot[tt % 4], (tt % 4) * 8)
            if tt >= 1:
                kb.wt("vector", S_tp, tp_done[tt - 1])
            xn_done[tt] = ln_vec(tt, uslot[tt % 4], ostage.ap()[:, 0:1024], acc_pro[tt], (tt % 4) * 8)
            if tt + 4 < NTP:
                kb.wt("scalar", SV, xn_done[tt])
                u_ms[tt + 4] = kb.dma("scalar", lambda t=tt + 4: nc.scalar.dma_start(
                    out=uslot[t % 4], in_=U[t * 128:(t + 1) * 128, :]), S_u[(tt + 4) % 4])
            kb.wt("tensor", SV, xn_done[tt])
            if tt >= 2:
                kb.wt("tensor", SV, xe_done[tt - 2])
            base = 2048 + (tt % 2) * 1024
            for dt in range(ND):
                kb.op("tensor", lambda d=dt, b=base: nc.tensor.transpose(
                    PS[:, b + d * 128: b + (d + 1) * 128],
                    ostage[:, d * 128:(d + 1) * 128], ident[:]),
                    inc=(S_tp if dt == ND - 1 else None))
            tp_done[tt] = S_tp.n
            kb.wt("vector", S_tp, tp_done[tt])
            w_ev = 128 if tt < NTT else 4
            for dt in range(ND):
                xe_done[tt] = kb.op("vector", lambda d=dt, t=tt, b=base, w=w_ev:
                                    nc.vector.tensor_copy(
                                        xnT[d][:, t * 128: t * 128 + w],
                                        PS[:, b + d * 128: b + d * 128 + w]))
        for q in range(4):
            emit_router_tcq(q)
            if n_experts and q in (1, 2):
                emit_A_chunk(0, 0, A00_PRO[2 * q - 2], bankR_free, 0)
                emit_A_chunk(0, 0, A00_PRO[2 * q - 1], bankR_free, 0)
        emit_wv_which(0, xe_done[NTP - 1])
        emit_wv_which(1, xe_done[NTP - 1])

        if stage == "xnt":
            _emit_dump(kb, nc, (SV, SS, S_out),
                       [xnT[dt].ap()[:, half * 1024:(half + 1) * 1024]
                        for dt in range(ND) for half in range(2)], ostage, OUT)
            kb.wt("sync", S_out, S_out.n)
            return nc, kb

        hev_done = SS.n

        lg_free = [max(xe_done[NTP - 1], wv_game_done[1]),
                   max(xe_done[NTP - 1], wv_game_done[1])]
        st_free = [xe_done[NTP - 2], xe_done[NTP - 2]]
        sel_done = {}
        stv_done = {}
        lgv_done = {}

        def emit_logits_tile(t):
            kb.wt("tensor", SS, hev_done)
            kb.wt("tensor", SV, max(pre_done, lg_free[t % 2]))
            bank = 4 + t % 2
            for dt in range(ND):
                kb.op("tensor", lambda d=dt, b=bank, t_=t: nc.tensor.matmul(
                    PS[:, b * 512: b * 512 + 8],
                    hview(d, t_ // 8)[:, (t_ % 8) * 128:(t_ % 8) * 128 + 128],
                    rw2bf[:, d * 8:(d + 1) * 8],
                    start=(d == 0), stop=(d == ND - 1)),
                    inc=(S_mm if dt == ND - 1 else None))
            kb.wt("vector", S_mm, S_mm.n)
            lgv_done[t] = kb.op("vector", lambda b=bank: nc.vector.tensor_tensor(
                lgt[:], PS[:, b * 512: b * 512 + 8], rb2r[:], op=ALU.add))
            lg_free[t % 2] = lgv_done[t]
            kb.op("vector", lambda: nc.vector.reduce_max(cs[:, 28:29], lgt[:], axis=AX.X))
            kb.op("vector", lambda: nc.vector.tensor_scalar(eqm[:], lgt[:], cs[:, 28:29], None, op0=ALU.is_ge))
            kb.op("vector", lambda: nc.vector.tensor_tensor(pr[:, 1:8], eqm[:, 1:8], eqm[:, 0:7], op=ALU.max))
            kb.op("vector", lambda: nc.vector.tensor_copy(pr[:, 0:1], eqm[:, 0:1]))
            kb.op("vector", lambda: nc.vector.tensor_tensor(lgt[:, 2:8], pr[:, 2:8], pr[:, 0:6], op=ALU.max))
            kb.op("vector", lambda: nc.vector.tensor_copy(lgt[:, 0:2], pr[:, 0:2]))
            kb.op("vector", lambda: nc.vector.tensor_tensor(pr[:, 4:8], lgt[:, 4:8], lgt[:, 0:4], op=ALU.max))
            kb.op("vector", lambda: nc.vector.tensor_copy(pr[:, 0:4], lgt[:, 0:4]))
            kb.op("vector", lambda: nc.vector.tensor_scalar(
                lgt[:, 0:7], pr[:, 0:7], -1.0, 1.0, op0=ALU.mult, op1=ALU.add))
            kb.op("vector", lambda t_=t: nc.vector.tensor_tensor(
                selm[:, t_ * 8 + 1: t_ * 8 + 8], eqm[:, 1:8], lgt[:, 0:7], op=ALU.mult))
            sel_done[t] = kb.op("vector", lambda t_=t: nc.vector.tensor_copy(
                selm[:, t_ * 8: t_ * 8 + 1], eqm[:, 0:1]))

        def emit_sel_transpose(t):
            kb.wt("tensor", SV, sel_done[t])
            if t < 2:
                kb.wt("tensor", SV, xe_done[NTP - 2])
            else:
                kb.wt("tensor", SS, stv_done[t - 2])
            sbank = 6 + t % 2
            kb.op("tensor", lambda t_=t, b=sbank: nc.tensor.transpose(
                PS[0:8, b * 512: b * 512 + 128], selm[:, t_ * 8:(t_ + 1) * 8], ident[:]),
                inc=S_tp)
            kb.wt("scalar", S_tp, S_tp.n)
            stv_done[t] = kb.op("scalar", lambda t_=t, b=sbank: nc.scalar.copy(
                selT[0:8, t_ * 128:(t_ + 1) * 128], PS[0:8, b * 512: b * 512 + 128]))

        be_done = 0
        be_bank = [0] * 8
        ginit_done_ss = 0
        ginit_done_sv = 0
        gi_free = [0, 0, 0, 0]

        def emit_ginit_tile(t, Ae):
            nonlocal ginit_done_ss, ginit_done_sv
            p = t % 4
            bp = 2 * p
            if t < 4:
                kb.wt("tensor", SV, b2_done)
                if p < 2:
                    kb.wt("tensor", SS, Ae)
                elif p == 2:
                    kb.wt("tensor", SV, lgv_done[NTT - 1])
                else:
                    kb.wt("tensor", SS, stv_done[NTT - 1])
            else:
                kb.wt("tensor", SS if t % 2 == 0 else SV, gi_free[p])
            kb.wt("tensor", SS, stv_done[t])
            kb.op("tensor", lambda t_=t, b=bp: nc.tensor.matmul(
                PS[:, b * 512:(b + 1) * 512], selT[0:8, t_ * 128:(t_ + 1) * 128],
                b2bf[:, 0:512], start=True, stop=True))
            kb.op("tensor", lambda t_=t, b=bp: nc.tensor.matmul(
                PS[:, (b + 1) * 512:(b + 2) * 512], selT[0:8, t_ * 128:(t_ + 1) * 128],
                b2bf[:, 512:1024], start=True, stop=True), inc=S_mm)
            if t % 2 == 0:
                kb.wt("scalar", S_mm, S_mm.n)
                gi_free[p] = kb.op("scalar", lambda t_=t, b=bp: nc.scalar.activation(
                    g[:, t_ * 1024:(t_ + 1) * 1024], PS[:, b * 512: b * 512 + 1024],
                    AF.Identity))
                ginit_done_ss = gi_free[p]
            else:
                kb.wt("vector", S_mm, S_mm.n)
                gi_free[p] = kb.op("vector", lambda t_=t, b=bp: nc.vector.tensor_copy(
                    g[:, t_ * 1024:(t_ + 1) * 1024], PS[:, b * 512: b * 512 + 1024]))
                ginit_done_sv = gi_free[p]

        if n_experts == 0:
            for t in range(NTT):
                emit_logits_tile(t)
                if t >= 1:
                    emit_sel_transpose(t - 1)
            emit_sel_transpose(NTT - 1)
            for t in range(NTT):
                emit_ginit_tile(t, 0)

        for ex in range(n_experts):
            for blk in range(2):
                if ex == 0 and blk == 0:
                    bankA = bankR_free
                    acs = [c for c in range(16) if c not in A00_PRO]
                else:
                    bankA = [0, 0, 0, 0]
                    acs = list(range(16))
                ent_Be = be_done
                if ex == 0 and blk == 0:
                    kb.wt("tensor", SV, wv_game_done[1])
                n_log = 0
                for ci, c in enumerate(acs):
                    emit_A_chunk(ex, blk, c, bankA, ent_Be if ci == 0 else 0)
                    if ex == 0 and blk == 0:
                        tgt = 2 * (ci + 1) if ci < 4 else ci + 5
                        tgt = min(tgt, NTT)
                        while n_log < tgt:
                            if n_log >= 1:
                                emit_sel_transpose(n_log - 1)
                            emit_logits_tile(n_log)
                            n_log += 1
                if ex == 0 and blk == 0:
                    emit_sel_transpose(NTT - 1)
                    Ae_snap = SS.n
                    for t in range(NTT):
                        emit_ginit_tile(t, Ae_snap)
                Ae_total = SS.n
                for dh in range(2):
                    mm_base = S_mm.n
                    for c in range(8):
                        for ft in range(4):
                            if ft % 2 == 0:
                                gh = use_chunk(pidx[("B", ex, blk, dh, c, ft // 2)])
                            if c == 0 and ft == 0:
                                kb.wt("tensor", SS, Ae_total)
                                if ex == 0 and blk == 0 and dh == 0:
                                    kb.wt("tensor", SV, ginit_done_sv)
                            f = 4 * c + ft
                            for t in range(8):
                                if c == 0 and ft == 0:
                                    kb.wt("tensor", SV, be_bank[t])
                                stop = f == NFT - 1
                                half_last = ft % 2 == 1 and t == 7
                                inc = None
                                if half_last:
                                    inc = S_wuse
                                elif stop:
                                    inc = S_mm
                                kb.op("tensor", lambda F_=f, t_=t, i=gh % 4, ftl=ft:
                                      nc.tensor.matmul(PS[:, t_ * 512:(t_ + 1) * 512],
                                                       hidA[F_][:, t_ * 128:(t_ + 1) * 128],
                                                       whb[i][:, (ftl % 2) * 512:(ftl % 2) * 512 + 512],
                                                       start=(F_ == 0), stop=(F_ == NFT - 1)),
                                      inc=inc)
                    assert S_mm.n == mm_base + 7
                    gc_last = pidx[("B", ex, blk, dh, 7, 1)]
                    if (ex, blk, dh) == (0, 0, 0):
                        kb.wt("vector", SS, ginit_done_ss)
                    for t in range(8):
                        if t == 7:
                            kb.wt("vector", S_wuse, wuse_after[gc_last])
                        else:
                            kb.wt("vector", S_mm, mm_base + t + 1)
                        tg = blk * 8 + t
                        be_done = kb.op("vector", lambda t_=t, tg_=tg, d=dh, E=ex:
                                        nc.vector.scalar_tensor_tensor(
                                            g[:, tg_ * 1024 + d * 512: tg_ * 1024 + d * 512 + 512],
                                            PS[:, t_ * 512:(t_ + 1) * 512],
                                            selm[:, tg_ * 8 + E: tg_ * 8 + E + 1],
                                            g[:, tg_ * 1024 + d * 512: tg_ * 1024 + d * 512 + 512],
                                            op0=ALU.mult, op1=ALU.add))
                        be_bank[t] = be_done

        if stage == "gexp":
            _emit_dump(kb, nc, (SV, SS, S_out),
                       [g.ap()[:, i * 1024:(i + 1) * 1024] for i in range(16)], ostage, OUT)
            kb.wt("sync", S_out, S_out.n)
            return nc, kb

        ensure(len(plan))
        wbt_done = max((v for s, v in cast_done[n_premm:] if s is SS), default=0)
        wbt_sv = max((v for s, v in cast_done[n_premm:] if s is SV), default=0)
        assert S_wuse.n == total_wuse
        kb.wt("scalar", S_wuse, total_wuse)
        kb.wt("scalar", SS, wbt_done)
        kb.wt("scalar", SV, wbt_sv)
        kb.dma("scalar", lambda: nc.scalar.dma_start(out=w1r, in_=W1R), S_sc)
        kb.dma("scalar", lambda: nc.scalar.dma_start(out=scrC, in_=W2R), S_sc)
        kb.dma("scalar", lambda: nc.scalar.dma_start(out=brep[:], in_=BREP), S_sc)
        pre_final_sc = S_sc.n
        kb.wt("vector", S_sc, pre_final_sc)
        kb.wt("vector", SS, wbt_done)
        kb.wt("vector", S_wuse, total_wuse)
        w2rb_done = kb.op("vector", lambda: nc.vector.tensor_copy(w2rb[:], scrC))

        fv_free = [0, 0]
        s1_done = {}
        y_done = {}
        o_done = {}
        acc_done = {}
        add_done = {}
        for tt in range(NTT):
            cb = (tt % 2) * 16
            if tt <= 1:
                kb.wt("sync", SV, xn_done[NTP - 1])
                kb.wt("sync", SS, acc_pro[NTP - 1 - tt])
            if tt >= 2:
                kb.wt("sync", SG, add_done[tt - 2])
                kb.wt("sync", SS, acc_done[tt - 2])
            u_ms[NTP + tt] = kb.dma("sync", lambda t=tt: nc.sync.dma_start(
                out=ustage[t % 2][:], in_=U[t * 128:(t + 1) * 128, :]), S_u[tt % 2])
            if tt >= 1:
                kb.wt("sync", SG, add_done[tt - 1])
                kb.dma("sync", lambda t=tt - 1: nc.sync.dma_start(
                    out=OUT[:, t * 1024:(t + 1) * 1024], in_=ostage[:]), S_out)
            kb.wt("scalar", S_u[tt % 2], u_ms[NTP + tt])
            acc_done[tt] = ln_acc(tt, ustage[tt % 2], cb)
            xfb = (tt % 2) * 4
            kb.wt("tensor", SV, fv_free[tt % 2])
            if tt == 0:
                kb.wt("tensor", SV, max(be_done, wbt_sv))
                kb.wt("tensor", SS, wbt_done)
            for which, woff, shift in ((0, 16, 0), (1, 24, 1)):
                for ec in range(2):
                    bank = xfb + which * 2 + ec
                    for dt in range(ND):
                        kb.op("tensor", lambda d=dt, b=bank, w=woff, E=ec, t=tt, sh=shift:
                              nc.tensor.matmul(PS[:, b * 512:(b + 1) * 512],
                                               xnT[d][:, t * 128 + sh: t * 128 + sh + 128],
                                               hidA[w + d][:, E * 512:(E + 1) * 512],
                                               start=(d == 0), stop=(d == ND - 1)),
                              inc=(S_mm if (which, ec, dt) == (1, 1, ND - 1) else None))
            PS_xf = PS[:, xfb * 512: xfb * 512 + 1024]
            PS_xb = PS[:, (xfb + 2) * 512: (xfb + 2) * 512 + 1024]
            P = scrC if tt % 2 == 0 else scrB
            kb.wt("scalar", S_mm, S_mm.n)
            if tt == 0:
                kb.wt("scalar", SV, w2rb_done)
            if tt == 1:
                kb.wt("scalar", SV, y_done[0])
            if tt >= 2:
                kb.wt("scalar", SG, o_done[tt - 2])
                kb.wt("scalar", SV, y_done[tt - 2])
            kb.op("scalar", lambda p=PS_xb, o=P: nc.scalar.activation(o, p, AF.Identity))
            xbcp = SS.n
            kb.wt("vector", S_mm, S_mm.n)
            if tt == 0:
                kb.wt("vector", SS, wbt_done)
                kb.wt("vector", S_wuse, total_wuse)
                kb.wt("vector", S_sc, pre_final_sc)
            interior = 0 < tt < NTT - 1
            if not interior:
                xn_fin = ln_vec(tt, ustage[tt % 2], scrA, acc_done[tt], cb)
            if tt == 0:
                kb.op("vector", lambda p=PS_xf: nc.vector.tensor_tensor(junkF, scrA, p, op=ALU.subtract))
                kb.op("vector", lambda p=PS_xf, t=tt: nc.vector.scalar_tensor_tensor(
                    junkF, junkF, m0[:, t:t + 1], p, op0=ALU.mult, op1=ALU.add))
                kb.op("vector", lambda: nc.vector.tensor_copy(scrB, junkF))
                xf_ap = scrB
            else:
                xf_ap = PS_xf
            if interior:
                gcol = (xfb + 2) * 512
                kb.wt("tensor", SS, xbcp)
                for which in range(2):
                    for dt in range(ND):
                        kb.op("tensor", lambda d=dt, w=which, t=tt, gc_=gcol:
                              nc.tensor.matmul(PS[:, gc_: gc_ + 1],
                                               xnT[d][:, t * 128 + w: t * 128 + w + 128],
                                               wvs[:, w * 8 + d: w * 8 + d + 1],
                                               start=(w == 0 and d == 0),
                                               stop=(w == 1 and d == ND - 1)),
                              inc=(S_mm if (which, dt) == (1, ND - 1) else None))
                gm_done = S_mm.n
                xn_fin = ln_vec(tt, ustage[tt % 2], scrA, acc_done[tt], cb)
                kb.wt("scalar", S_mm, gm_done)
                fsig = kb.op("scalar", lambda gc_=gcol, c=cb: nc.scalar.activation(
                    cs[:, c + 10:c + 11], PS[:, gc_: gc_ + 1], AF.Sigmoid, bias=brep[:]))
            else:
                kb.op("vector", lambda x=xf_ap: nc.vector.tensor_tensor(junkF, x, w1r, op=ALU.mult))
                kb.op("vector", lambda c=cb: nc.vector.reduce_sum(cs[:, c + 8:c + 9], junkF, axis=AX.X))
                kb.wt("vector", SS, xbcp)
                if tt == NTT - 1:
                    kb.op("vector", lambda o=P: nc.vector.tensor_tensor(junkF, scrA, o, op=ALU.subtract))
                    kb.op("vector", lambda o=P, t=tt: nc.vector.scalar_tensor_tensor(
                        o, junkF, mL[:, t:t + 1], o, op0=ALU.mult, op1=ALU.add))
                kb.op("vector", lambda o=P: nc.vector.tensor_tensor(junkF, o, w2rb[:], op=ALU.mult))
                kb.op("vector", lambda c=cb: nc.vector.reduce_sum(cs[:, c + 9:c + 10], junkF, axis=AX.X))
                fg = kb.op("vector", lambda c=cb: nc.vector.tensor_tensor(
                    cs[:, c + 10:c + 11], cs[:, c + 8:c + 9], cs[:, c + 9:c + 10], op=ALU.add))
                kb.wt("scalar", SV, fg)
                fsig = kb.op("scalar", lambda c=cb: nc.scalar.activation(
                    cs[:, c + 10:c + 11], cs[:, c + 10:c + 11], AF.Sigmoid, bias=brep[:]))
            kb.wt("scalar", S_out, 16 * tt)
            if tt >= 1:
                kb.wt("scalar", SG, add_done[tt - 1])
            fsil = kb.op("scalar", lambda t=tt: nc.scalar.activation(
                ostage[:], g[:, t * 1024:(t + 1) * 1024], AF.Sigmoid))
            kb.wt("vector", SS, fsig)
            s1_done[tt] = kb.op("vector", lambda x=xf_ap, o=P: nc.vector.tensor_tensor(o, x, o, op=ALU.add))
            fv_free[tt % 2] = s1_done[tt]
            kb.op("vector", lambda o=P: nc.vector.tensor_tensor(o, o, scrA, op=ALU.subtract))
            y_done[tt] = kb.op("vector", lambda c=cb, o=P: nc.vector.scalar_tensor_tensor(
                o, o, cs[:, c + 10:c + 11], scrA, op0=ALU.mult, op1=ALU.add))
            kb.wt("gpsimd", SS, fsil)
            kb.wt("gpsimd", SV, y_done[tt])
            kb.wt("gpsimd", S_u[tt % 2], u_ms[NTP + tt])
            kb.op("gpsimd", lambda t=tt: nc.gpsimd.tensor_tensor(
                ostage[:], ostage[:], g[:, t * 1024:(t + 1) * 1024], op=ALU.mult))
            o_done[tt] = kb.op("gpsimd", lambda o=P: nc.gpsimd.tensor_tensor(
                ostage[:], o, ostage[:], op=ALU.mult))
            add_done[tt] = kb.op("gpsimd", lambda t=tt: nc.gpsimd.tensor_tensor(
                ostage[:], ostage[:], ustage[t % 2][:], op=ALU.add))

        kb.wt("sync", SG, add_done[NTT - 1])
        kb.dma("sync", lambda t=NTT - 1: nc.sync.dma_start(
            out=OUT[:, t * 1024:(t + 1) * 1024], in_=ostage[:]), S_out)
        kb.wt("sync", S_out, S_out.n)
        return nc, kb


def _emit_dump(kb, nc, sems, aps, ostage, OUT):
    SV, SS, S_out = sems
    for i, ap in enumerate(aps[:16]):
        if i >= 1:
            kb.wt("vector", S_out, 16 * i)
        w = ap.shape[-1]
        if w < 1024:
            kb.op("vector", lambda: nc.vector.memset(ostage[:], 0.0))
        cp = kb.op("vector", lambda a=ap, W=w: nc.vector.tensor_copy(ostage[:, 0:W], a))
        kb.wt("scalar", SV, cp)
        kb.dma("scalar", lambda j=i: nc.scalar.dma_start(
            out=OUT[:, j * 1024:(j + 1) * 1024], in_=ostage[:]), S_out)


def _run(nc, kb):
    with nc.Block() as block:
        @block.sync
        def _(eng):
            kb.run("sync")

        @block.scalar
        def _(eng):
            kb.run("scalar")

        @block.vector
        def _(eng):
            kb.run("vector")

        @block.tensor
        def _(eng):
            kb.run("tensor")

        @block.gpsimd
        def _(eng):
            kb.run("gpsimd")


def build_nc(stage="full", n_experts=NE):
    nc, kb = _build(stage, n_experts)
    _run(nc, kb)
    return nc


def host_inputs(inputs):
    f32 = np.float32
    u = np.ascontiguousarray(np.asarray(inputs["L0_attn_u"], f32))
    W1 = np.asarray(inputs["L0_ffn_expert_w1"], f32)
    W2 = np.asarray(inputs["L0_ffn_expert_w2"], f32)
    rw1 = np.asarray(inputs["L0_ffn_router_w1"], f32)
    rw2 = np.asarray(inputs["L0_ffn_router_w2"], f32)
    rb1 = np.asarray(inputs["L0_ffn_router_b1"], f32)
    rb2 = np.asarray(inputs["L0_ffn_router_b2"], f32)
    b1 = np.asarray(inputs["L0_ffn_expert_b1"], f32)
    b2 = np.asarray(inputs["L0_ffn_expert_b2"], f32)
    WB = np.asarray(inputs["L0_attn_WB"], f32)
    WBf = np.asarray(inputs["L0_attn_WBf"], f32)
    w1 = np.asarray(inputs["L0_attn_w1"], f32)
    w2 = np.asarray(inputs["L0_attn_w2"], f32)
    b = np.asarray(inputs["L0_attn_b"], f32)

    shared = {
        "W1L": np.ascontiguousarray(
            W1.reshape(8, 8, 128, 16, 2, 128).transpose(0, 3, 2, 4, 1, 5).reshape(8, 16, 128, 2048)),
        "W2L": np.ascontiguousarray(
            W2.reshape(8, 8, 4, 128, 2, 512).transpose(0, 4, 1, 3, 2, 5).reshape(8, 2, 8, 128, 2048)),
        "RW1L": np.ascontiguousarray(
            rw1.reshape(8, 128, 4, 2, 128).transpose(2, 1, 3, 0, 4).reshape(4, 128, 2048)),
        "WBTL": np.ascontiguousarray(
            WB.T.reshape(4, 2, 128, 1024).transpose(0, 2, 1, 3).reshape(4, 128, 2048)),
        "WBFTL": np.ascontiguousarray(
            WBf.T.reshape(4, 2, 128, 1024).transpose(0, 2, 1, 3).reshape(4, 128, 2048)),
        "RW2L": np.ascontiguousarray(rw2.reshape(8, 128, 8).transpose(1, 0, 2).reshape(128, 64)),
        "B1L": np.ascontiguousarray(b1.reshape(8, 32, 128).transpose(2, 0, 1).reshape(128, 256)),
        "B2L": np.ascontiguousarray(b2),
        "RB1L": np.ascontiguousarray(rb1.reshape(8, 128).T),
        "RB2R": np.ascontiguousarray(np.tile(rb2[None, :], (128, 1))),
        "W1R": np.ascontiguousarray(np.tile(w1.reshape(1, 1024), (128, 1))),
        "W2R": np.ascontiguousarray(np.tile(w2.reshape(1, 1024), (128, 1))),
        "BREP": np.full((128, 1), b.reshape(-1)[0], f32),
        "IDENT": np.eye(128, dtype=f32),
        "WBL": np.ascontiguousarray(WB.reshape(8, 128, 1024)),
        "WBFL": np.ascontiguousarray(WBf.reshape(8, 128, 1024)),
        "W1C": np.ascontiguousarray(w1.reshape(8, 128).T),
        "W2C": np.ascontiguousarray(w2.reshape(8, 128).T),
    }
    in_maps = []
    for c in range(8):
        bb, h = c // 2, c % 2
        if h == 0:
            shard = u[bb, 0:2049]
        else:
            shard = np.concatenate([u[bb, 2048:4096], np.zeros((1, 1024), f32)], 0)
        shard = np.concatenate([shard, np.zeros((2176 - 2049, 1024), f32)], 0)
        msk0 = np.zeros(2048, f32)
        mskL = np.zeros(2048, f32)
        if h == 0:
            msk0[0] = 1.0
        else:
            mskL[2047] = 1.0
        m = dict(shared)
        m["U"] = np.ascontiguousarray(shard)
        m["M0"] = np.ascontiguousarray(msk0.reshape(16, 128).T)
        m["ML"] = np.ascontiguousarray(mskL.reshape(16, 128).T)
        in_maps.append(m)
    return in_maps


def assemble(results):
    outs = []
    for c in range(8):
        o = results[c]["OUT"].reshape(128, 16, 1024).transpose(1, 0, 2).reshape(2048, 1024)
        outs.append(o)
    return np.stack([np.concatenate([outs[2 * bb], outs[2 * bb + 1]], 0)
                     for bb in range(4)], 0).astype(np.float32)


def kernel(**inputs):
    nc = build_nc("full", NE)
    in_maps = host_inputs(inputs)
    res = run_bass_kernel_spmd(nc, in_maps, core_ids=list(range(8)))
    return assemble(res.results)


# version 21
# speedup vs baseline: 1.0251x; 1.0251x over previous
import numpy as np
import ml_dtypes
from contextlib import ExitStack

import concourse.bass as bass
import concourse.mybir as mybir
from concourse.bass_utils import run_bass_kernel_spmd

F32 = mybir.dt.float32
BF16 = mybir.dt.bfloat16
AF = mybir.ActivationFunctionType
ALU = mybir.AluOpType
AX = mybir.AxisListType

T = 2048
NTT = 16
NTP = 17
D = 1024
ND = 8
FF = 4096
NFT = 32
NE = 8
EPS = 1e-5
ENGINES = ("sync", "scalar", "vector", "tensor", "gpsimd")
SELF_SYNC = {"vector", "scalar", "gpsimd"}


class _Sem:
    __slots__ = ("h", "n", "waited")

    def __init__(self, h):
        self.h = h
        self.n = 0
        self.waited = {}


class _KB:
    def __init__(self, nc, esem):
        self.nc = nc
        self.q = {e: [] for e in ENGINES}
        self.esem = esem

    def op(self, eng, fn, inc=None, by=1):
        if eng in SELF_SYNC:
            assert inc is None
            es = self.esem[eng]
            self.wt(eng, es, es.n)
            inc, by = es, 1
        if inc is not None:
            inc.n += by
        self.q[eng].append((fn, inc, by))
        return inc.n if inc is not None else None

    def dma(self, eng, fn, inc):
        inc.n += 16
        self.q[eng].append((fn, inc, 16))
        return inc.n

    def wt(self, eng, sem, val):
        if val <= 0:
            return
        assert val <= sem.n, f"wait {eng} for {val} but planned only {sem.n}"
        self._wt(eng, sem, val)

    def wt_future(self, eng, sem, val):
        if val <= 0:
            return
        self._wt(eng, sem, val)

    def _wt(self, eng, sem, val):
        if val <= sem.waited.get(eng, 0):
            return
        sem.waited[eng] = val
        nc = self.nc

        def f(e=eng, s=sem, v=val):
            return getattr(nc, e).wait_ge(s.h, v)
        self.q[eng].append((f, None, 0))

    def run(self, eng):
        for fn, inc, by in self.q[eng]:
            r = fn()
            if inc is not None:
                r.then_inc(inc.h, by)


def _build(stage="full", n_experts=NE):
    nc = bass.Bass("TRN2", target_bir_lowering=False)

    def din(name, shape):
        return nc.dram_tensor(name, shape, F32, kind="ExternalInput").ap()

    U = din("U", [2176, 1024])
    W1L = din("W1L", [NE, 16, 128, 2048])
    W2L = din("W2L", [NE, 2, 8, 128, 2048])
    RW1L = din("RW1L", [4, 128, 2048])
    WBTL = din("WBTL", [4, 128, 2048])
    WBFTL = din("WBFTL", [4, 128, 2048])
    RW2L = din("RW2L", [128, 64])
    B1L = din("B1L", [128, 256])
    B2L = din("B2L", [8, 1024])
    RB1L = din("RB1L", [128, 8])
    RB2R = din("RB2R", [128, 8])
    W1R = din("W1R", [128, 1024])
    W2R = din("W2R", [128, 1024])
    BREP = din("BREP", [128, 1])
    M0 = din("M0", [128, 16])
    ML = din("ML", [128, 16])
    IDENT = din("IDENT", [128, 128])
    WBL = din("WBL", [8, 128, 1024])
    WBFL = din("WBFL", [8, 128, 1024])
    W1C = din("W1C", [128, 8])
    W2C = din("W2C", [128, 8])
    OUT = nc.dram_tensor("OUT", [128, 16384], F32, kind="ExternalOutput").ap()

    with ExitStack() as stack:
        e = stack.enter_context
        xnT = [e(nc.sbuf_tensor(f"xnT{i}", [128, 2052], BF16)) for i in range(ND)]
        hidA = [e(nc.sbuf_tensor(f"hid{i}", [128, 1024], BF16)) for i in range(32)]
        g = e(nc.sbuf_tensor("g", [128, 16384], F32))
        wstage = [e(nc.sbuf_tensor(f"wst{i}", [128, 2048], F32)) for i in range(2)]
        wbf = [e(nc.sbuf_tensor(f"wbf{i}", [128, 2048], BF16)) for i in range(2)]
        ustage = [e(nc.sbuf_tensor(f"ust{i}", [128, 1024], F32)) for i in range(2)]
        ostage = e(nc.sbuf_tensor("ost", [128, 1024], F32))
        b1sb = e(nc.sbuf_tensor("b1sb", [128, 256], F32))
        rb1 = e(nc.sbuf_tensor("rb1", [128, 8], F32))
        rb2r = e(nc.sbuf_tensor("rb2r", [128, 8], F32))
        rw2sb = e(nc.sbuf_tensor("rw2sb", [128, 64], F32))
        rw2bf = e(nc.sbuf_tensor("rw2bf", [128, 64], BF16))
        b2bf = e(nc.sbuf_tensor("b2bf", [8, 1024], BF16))
        selT = e(nc.sbuf_tensor("selT", [8, 2048], BF16))
        selm = e(nc.sbuf_tensor("selm", [128, 128], F32))
        m0 = e(nc.sbuf_tensor("m0", [128, 16], F32))
        mL = e(nc.sbuf_tensor("mL", [128, 16], F32))
        ident = e(nc.sbuf_tensor("ident", [128, 128], F32))
        brep = e(nc.sbuf_tensor("brep", [128, 1], F32))
        w2rb = e(nc.sbuf_tensor("w2rb", [128, 1024], BF16))
        w1cb = e(nc.sbuf_tensor("w1cb", [128, 8], BF16))
        w2cb = e(nc.sbuf_tensor("w2cb", [128, 8], BF16))
        w12c = e(nc.sbuf_tensor("w12c", [128, 16], F32))
        wvs = e(nc.sbuf_tensor("wvs", [128, 16], BF16))
        mhalf = e(nc.sbuf_tensor("mhalf", [128, 1], F32))
        cs = e(nc.sbuf_tensor("cs", [128, 32], F32))
        lgt = e(nc.sbuf_tensor("lgt", [128, 8], F32))
        pr = e(nc.sbuf_tensor("pr", [128, 8], F32))
        eqm = e(nc.sbuf_tensor("eqm", [128, 8], F32))
        PS = e(nc.psum_tensor("PS", [128, 4096], F32))

        SV = _Sem(e(nc.semaphore("SV")))
        SS = _Sem(e(nc.semaphore("SS")))
        SG = _Sem(e(nc.semaphore("SG")))
        S_sc = _Sem(e(nc.semaphore("S_sc")))
        S_wd = [_Sem(e(nc.semaphore(f"S_wd{i}"))) for i in range(4)]
        S_u = [_Sem(e(nc.semaphore(f"S_u{i}"))) for i in range(4)]
        S_out = _Sem(e(nc.semaphore("S_out")))
        S_tp = _Sem(e(nc.semaphore("S_tp")))
        S_mm = _Sem(e(nc.semaphore("S_mm")))
        S_wuse = _Sem(e(nc.semaphore("S_wuse")))

        esem = {"vector": SV, "scalar": SS, "gpsimd": SG}
        kb = _KB(nc, esem)
        scrA = wstage[0].ap()[:, 0:1024]
        scrB = wstage[0].ap()[:, 1024:2048]
        scrC = wstage[1].ap()[:, 0:1024]
        w1r = wstage[1].ap()[:, 1024:2048]
        junkF = wbf[1].ap().bitcast(F32)[:, 0:1024]
        jnk = hidA[0].ap().bitcast(F32)
        hc = g.ap().bitcast(BF16)

        def hview(dd, half):
            return hc[:, (2 * dd + half) * 1024:(2 * dd + half) * 1024 + 1024]

        for dst, src in [(ident, IDENT), (m0, M0), (mL, ML), (b1sb, B1L),
                         (rb1, RB1L), (rb2r, RB2R), (rw2sb, RW2L)]:
            kb.dma("scalar", lambda d=dst, s=src: nc.scalar.dma_start(out=d[:], in_=s), S_sc)
        kb.dma("scalar", lambda: nc.scalar.dma_start(out=ustage[0][0:8, :], in_=B2L), S_sc)
        kb.dma("scalar", lambda: nc.scalar.dma_start(out=w12c[:, 0:8], in_=W1C), S_sc)
        kb.dma("scalar", lambda: nc.scalar.dma_start(out=w12c[:, 8:16], in_=W2C), S_sc)
        n_small = 10
        kb.wt("vector", S_sc, 16 * n_small)
        b2_done = kb.op("vector", lambda: nc.vector.tensor_copy(b2bf[:], ustage[0][0:8, :]))
        kb.op("vector", lambda: nc.vector.tensor_copy(w1cb[:], w12c[:, 0:8]))
        w12_done = kb.op("vector", lambda: nc.vector.tensor_copy(w2cb[:], w12c[:, 8:16]))
        pre_done = kb.op("vector", lambda: nc.vector.tensor_copy(rw2bf[:], rw2sb[:]))
        kb.op("gpsimd", lambda: nc.gpsimd.memset(mhalf[:], -0.5))

        def ln_acc(tt, usrc, cb):
            for half, col in ((0, cb + 0), (1, cb + 1)):
                kb.op("scalar", lambda h=half, c=col, u=usrc: nc.scalar.activation(
                    jnk[:], u[:, h * 512:(h + 1) * 512], AF.Identity,
                    accum_out=cs[:, c:c + 1]))
            for half, col in ((0, cb + 2), (1, cb + 3)):
                acc = kb.op("scalar", lambda h=half, c=col, u=usrc: nc.scalar.activation(
                    jnk[:], u[:, h * 512:(h + 1) * 512], AF.Square,
                    accum_out=cs[:, c:c + 1]))
            return acc

        def ln_vec(tt, usrc, xn_out, acc, cb):
            kb.wt("vector", SS, acc)
            kb.op("vector", lambda c=cb: nc.vector.tensor_tensor(
                cs[:, c + 4:c + 5], cs[:, c:c + 1], cs[:, c + 1:c + 2], op=ALU.add))
            kb.op("vector", lambda c=cb: nc.vector.tensor_scalar(
                cs[:, c + 4:c + 5], cs[:, c + 4:c + 5], 1.0 / D, None, op0=ALU.mult))
            kb.op("vector", lambda c=cb: nc.vector.tensor_tensor(
                cs[:, c + 5:c + 6], cs[:, c + 4:c + 5], cs[:, c + 4:c + 5], op=ALU.mult))
            kb.op("vector", lambda c=cb: nc.vector.tensor_tensor(
                cs[:, c + 6:c + 7], cs[:, c + 2:c + 3], cs[:, c + 3:c + 4], op=ALU.add))
            kb.op("vector", lambda c=cb: nc.vector.tensor_scalar(
                cs[:, c + 6:c + 7], cs[:, c + 6:c + 7], 1.0 / D, cs[:, c + 5:c + 6],
                op0=ALU.mult, op1=ALU.subtract))
            vpe = kb.op("vector", lambda c=cb: nc.vector.tensor_scalar(
                cs[:, c + 6:c + 7], cs[:, c + 6:c + 7], EPS, None, op0=ALU.add))
            kb.wt("gpsimd", SV, vpe)
            pw = kb.op("gpsimd", lambda c=cb: nc.gpsimd.tensor_tensor(
                cs[:, c + 7:c + 8], cs[:, c + 6:c + 7], mhalf[:], op=ALU.pow))
            kb.wt("vector", SG, pw)
            return kb.op("vector", lambda c=cb, u=usrc, o=xn_out: nc.vector.tensor_scalar(
                o, u[:], cs[:, c + 4:c + 5], cs[:, c + 7:c + 8],
                op0=ALU.subtract, op1=ALU.mult))

        plan = []
        pidx = {}

        def padd(key, src, eng):
            pidx[key] = len(plan)
            plan.append((src, eng, True, None))

        def padd_R(tcq):
            for c in range(4):
                for h in range(2):
                    padd(("R", tcq, c, h), RW1L[c, :, h * 1024:(h + 1) * 1024],
                         "vector" if h == 0 else "scalar")

        def padd_A(ex, blk, c):
            for h in range(2):
                padd(("A", ex, blk, c, h), W1L[ex, c, :, h * 1024:(h + 1) * 1024],
                     "vector" if h == 0 else "scalar")

        A00_PRO = (2, 3, 4, 5) if n_experts else ()
        padd_R(0)
        padd_R(1)
        for _c in A00_PRO[:2]:
            padd_A(0, 0, _c)
        padd_R(2)
        for _c in A00_PRO[2:]:
            padd_A(0, 0, _c)
        padd_R(3)
        for _w, _src in ((0, WBL), (1, WBFL)):
            for dt in range(ND):
                padd(("wv", _w, dt), _src[dt], "vector" if dt % 2 == 0 else "scalar")
        for _ex in range(n_experts):
            for _blk in range(2):
                for c in range(16):
                    if (_ex, _blk) == (0, 0) and c in A00_PRO:
                        continue
                    padd_A(_ex, _blk, c)
                for dh in range(2):
                    for c in range(8):
                        for h in range(2):
                            padd(("B", _ex, _blk, dh, c, h),
                                 W2L[_ex, dh, c, :, h * 1024:(h + 1) * 1024],
                                 "vector" if h == 0 else "scalar")
        n_premm = len(plan)
        for c in range(4):
            for h in range(2):
                plan.append((WBTL[c, :, h * 1024:(h + 1) * 1024],
                             ("vector" if h == 0 else "scalar"), False, 16 + 2 * c + h))
        for c in range(4):
            for h in range(2):
                plan.append((WBFTL[c, :, h * 1024:(h + 1) * 1024],
                             ("vector" if h == 0 else "scalar"), False, 24 + 2 * c + h))
        wuse_after = []
        acc_w = 0
        for p in plan:
            acc_w += 1 if p[2] else 0
            wuse_after.append(acc_w)
        total_wuse = acc_w
        cast_done = []
        emitted = [0]
        whf = [wstage[i][:, o * 1024:(o + 1) * 1024] for i in range(2) for o in range(2)]
        whb = [wbf[i][:, o * 1024:(o + 1) * 1024] for i in range(2) for o in range(2)]

        def ensure(upto):
            while emitted[0] < min(upto, len(plan)):
                j = emitted[0]
                src, eng, has_mm, wdst = plan[j]
                slot = j % 4
                if j >= 4:
                    sem, val = cast_done[j - 4]
                    kb.wt("sync", sem, val)
                kb.dma("sync", lambda s=src, i=slot: nc.sync.dma_start(
                    out=whf[i], in_=s), S_wd[slot])
                kb.wt(eng, S_wd[slot], 16 * (j // 4 + 1))
                if j >= 4:
                    kb.wt_future(eng, S_wuse, wuse_after[j - 4])
                dst = whb[slot] if wdst is None else hidA[wdst][:]
                if wdst is not None:
                    kb.wt_future(eng, S_wuse, wuse_after[n_premm - 1])
                if eng == "vector":
                    kb.op(eng, lambda d=dst, i=slot: nc.vector.tensor_copy(d, whf[i]))
                else:
                    kb.op(eng, lambda d=dst, i=slot: nc.scalar.copy(d, whf[i]))
                cast_done.append((esem[eng], esem[eng].n))
                emitted[0] += 1

        def use_chunk(j):
            ensure(j + 5)
            kb.wt("tensor", cast_done[j][0], cast_done[j][1])
            return j

        wvrow = ostage.ap()[0:1, 0:512]
        wv_game_done = [0, 0]

        def emit_wv_which(which, xe_gate):
            wc = w1cb if which == 0 else w2cb
            kb.wt("tensor", SV, w12_done)
            kb.wt("tensor", SS, max(bankR_free))
            for dt in range(ND):
                gh = use_chunk(pidx[("wv", which, dt)])
                for nch in range(2):
                    bank = which * 2 + nch
                    kb.op("tensor", lambda d=dt, b=bank, n_=nch, i=gh % 4, w=wc:
                          nc.tensor.matmul(PS[0:1, b * 512:(b + 1) * 512],
                                           w[:, d:d + 1],
                                           whb[i][:, n_ * 512:(n_ + 1) * 512],
                                           start=(d == 0), stop=(d == ND - 1)),
                          inc=(S_wuse if nch == 1 else None))
            kb.wt("vector", S_wuse, wuse_after[pidx[("wv", which, ND - 1)]])
            kb.wt("vector", S_tp, tp_done[NTP - 1])
            kb.wt("tensor", SV, xe_gate)
            kb.wt("tensor", SS, xe_done_s[NTP - 1])
            for nch in range(2):
                bank = which * 2 + nch
                kb.op("vector", lambda b=bank: nc.vector.tensor_copy(
                    wvrow[:, :], PS[0:1, b * 512:(b + 1) * 512]))
                rowcp = SV.n
                for k in range(4):
                    kb.wt("tensor", SV, rowcp)
                    kb.op("tensor", lambda K=k: nc.tensor.transpose(
                        PS[:, 2048 + K: 2048 + K + 1],
                        wvrow[0:1, K * 128:(K + 1) * 128], ident[0:1, 0:1]),
                        inc=(S_tp if k == 3 else None))
                kb.wt("vector", S_tp, S_tp.n)
                for k in range(4):
                    kb.op("vector", lambda K=k, w=which, n_=nch: nc.vector.tensor_copy(
                        wvs[:, w * 8 + n_ * 4 + K: w * 8 + n_ * 4 + K + 1],
                        PS[:, 2048 + K: 2048 + K + 1]))
            wv_game_done[which] = SV.n

        bankR_free = [0, 0, 0, 0]
        grpq = [0]

        def emit_router_tcq(tcq):
            kb.wt("scalar", S_sc, 16 * n_small)
            kb.wt("tensor", SV, max(wv_game_done))
            for c in range(4):
                for sub in range(2):
                    gh = use_chunk(pidx[("R", tcq, c, sub)])
                    kb.wt("tensor", SV, xe_done[tcq * 4 + 3])
                    kb.wt("tensor", SS, xe_done_s[tcq * 4 + 3])
                    dd = 2 * c + sub
                    bank = grpq[0] % 4
                    grpq[0] += 1
                    kb.wt("tensor", SS, bankR_free[bank])
                    for dt in range(ND):
                        kb.op("tensor", lambda d=dt, b=bank, q=tcq, i=gh % 4:
                              nc.tensor.matmul(PS[:, b * 512:(b + 1) * 512],
                                               whb[i][:, d * 128:(d + 1) * 128],
                                               xnT[d][:, q * 512:(q + 1) * 512],
                                               start=(d == 0), stop=(d == ND - 1)),
                              inc=(S_wuse if dt == ND - 1 else None))
                    kb.wt("scalar", S_wuse, wuse_after[gh])
                    kb.op("scalar", lambda d=dd, q=tcq, b=bank: nc.scalar.activation(
                        hview(d, q // 2)[:, (q % 2) * 512:(q % 2) * 512 + 512],
                        PS[:, b * 512:(b + 1) * 512], AF.Relu, bias=rb1[:, d:d + 1]))
                    bankR_free[bank] = SS.n

        def emit_A_chunk(ex, blk, c, bankA, ent_Be):
            kb.wt("tensor", SV, xe_done[7 if blk == 0 else NTT - 1])
            kb.wt("tensor", SS, xe_done_s[7 if blk == 0 else NTT - 1])
            if (ex, blk, c) == (0, 0, A00_PRO[0]) and n_experts:
                kb.wt("scalar", SV, wv_game_done[1])
            for sub in range(2):
                gh = use_chunk(pidx[("A", ex, blk, c, sub)])
                fp = 2 * c + sub
                for tc in range(2):
                    bank = (fp % 2) * 2 + tc
                    if ent_Be:
                        kb.wt("tensor", SV, be_bank[bank])
                    kb.wt("tensor", SS, bankA[bank])
                    for dt in range(ND):
                        chunk_last = dt == ND - 1 and tc == 1
                        kb.op("tensor", lambda d=dt, b=bank, q=tc, i=gh % 4, bl=blk:
                              nc.tensor.matmul(PS[:, b * 512:(b + 1) * 512],
                                               whb[i][:, d * 128:(d + 1) * 128],
                                               xnT[d][:, bl * 1024 + q * 512: bl * 1024 + q * 512 + 512],
                                               start=(d == 0), stop=(d == ND - 1)),
                              inc=(S_wuse if chunk_last else None))
                kb.wt("scalar", S_wuse, wuse_after[gh])
                pair = (fp % 2) * 2
                kb.op("scalar", lambda f=fp, p=pair, E=ex: nc.scalar.activation(
                    hidA[f][:], PS[:, p * 512: p * 512 + 1024], AF.Relu,
                    bias=b1sb[:, E * 32 + f: E * 32 + f + 1]))
                bankA[pair] = bankA[pair + 1] = SS.n

        uslot = [ustage[0].ap()[:, :], ustage[1].ap()[:, :],
                 g.ap()[:, 8192:9216], g.ap()[:, 9216:10240]]
        xnbuf = [ostage.ap()[:, 0:1024], g.ap()[:, 10240:11264]]
        xn_done = {}
        tp_done = {}
        xe_done = {}
        xe_done_s = {}
        acc_pro = {}
        u_ms = {}
        def emit_xe(t):
            base_t = 2048 + (t % 2) * 1024
            w_ev = 128 if t < NTT else 4
            kb.wt("vector", S_tp, tp_done[t])
            kb.wt("scalar", S_tp, tp_done[t])
            for dt in range(4):
                xe_done[t] = kb.op("vector", lambda d=dt, t_=t, b=base_t, w=w_ev:
                                   nc.vector.tensor_copy(
                                       xnT[d][:, t_ * 128: t_ * 128 + w],
                                       PS[:, b + d * 128: b + d * 128 + w]))
            for dt in range(4, ND):
                xe_done_s[t] = kb.op("scalar", lambda d=dt, t_=t, b=base_t, w=w_ev:
                                     nc.scalar.copy(
                                         xnT[d][:, t_ * 128: t_ * 128 + w],
                                         PS[:, b + d * 128: b + d * 128 + w]))

        kb.wt("scalar", SV, pre_done)
        for t in range(4):
            u_ms[t] = kb.dma("scalar", lambda t_=t: nc.scalar.dma_start(
                out=uslot[t_], in_=U[t_ * 128:(t_ + 1) * 128, :]), S_u[t])
        for tt in range(NTP):
            if tt == 0:
                kb.wt("tensor", S_sc, 16 * n_small)
            kb.wt("scalar", S_u[tt % 4], u_ms[tt])
            acc_pro[tt] = ln_acc(tt, uslot[tt % 4], (tt % 4) * 8)
            if tt >= 2:
                kb.wt("vector", S_tp, tp_done[tt - 2])
            xn_done[tt] = ln_vec(tt, uslot[tt % 4], xnbuf[tt % 2], acc_pro[tt], (tt % 4) * 8)
            if tt + 4 < NTP:
                kb.wt("scalar", SV, xn_done[tt])
                u_ms[tt + 4] = kb.dma("scalar", lambda t=tt + 4: nc.scalar.dma_start(
                    out=uslot[t % 4], in_=U[t * 128:(t + 1) * 128, :]), S_u[(tt + 4) % 4])
            kb.wt("tensor", SV, xn_done[tt])
            if tt >= 2:
                kb.wt("tensor", SV, xe_done[tt - 2])
                kb.wt("tensor", SS, xe_done_s[tt - 2])
            base = 2048 + (tt % 2) * 1024
            for dt in range(ND):
                kb.op("tensor", lambda d=dt, b=base, o=xnbuf[tt % 2]: nc.tensor.transpose(
                    PS[:, b + d * 128: b + (d + 1) * 128],
                    o[:, d * 128:(d + 1) * 128], ident[:]),
                    inc=(S_tp if dt == ND - 1 else None))
            tp_done[tt] = S_tp.n
            if tt >= 1:
                emit_xe(tt - 1)
        emit_xe(NTP - 1)
        for q in range(4):
            emit_router_tcq(q)
            if n_experts and q in (1, 2):
                emit_A_chunk(0, 0, A00_PRO[2 * q - 2], bankR_free, 0)
                emit_A_chunk(0, 0, A00_PRO[2 * q - 1], bankR_free, 0)
        emit_wv_which(0, xe_done[NTP - 1])
        emit_wv_which(1, xe_done[NTP - 1])

        if stage == "xnt":
            _emit_dump(kb, nc, (SV, SS, S_out),
                       [xnT[dt].ap()[:, half * 1024:(half + 1) * 1024]
                        for dt in range(ND) for half in range(2)], ostage, OUT)
            kb.wt("sync", S_out, S_out.n)
            return nc, kb

        hev_done = SS.n

        lg_free = [max(xe_done[NTP - 1], wv_game_done[1]),
                   max(xe_done[NTP - 1], wv_game_done[1])]
        st_free = [xe_done[NTP - 2], xe_done[NTP - 2]]
        sel_done = {}
        stv_done = {}
        lgv_done = {}

        def emit_logits_tile(t):
            kb.wt("tensor", SS, max(hev_done, xe_done_s[NTP - 1]))
            kb.wt("tensor", SV, max(pre_done, lg_free[t % 2]))
            bank = 4 + t % 2
            for dt in range(ND):
                kb.op("tensor", lambda d=dt, b=bank, t_=t: nc.tensor.matmul(
                    PS[:, b * 512: b * 512 + 8],
                    hview(d, t_ // 8)[:, (t_ % 8) * 128:(t_ % 8) * 128 + 128],
                    rw2bf[:, d * 8:(d + 1) * 8],
                    start=(d == 0), stop=(d == ND - 1)),
                    inc=(S_mm if dt == ND - 1 else None))
            kb.wt("vector", S_mm, S_mm.n)
            lgv_done[t] = kb.op("vector", lambda b=bank: nc.vector.tensor_tensor(
                lgt[:], PS[:, b * 512: b * 512 + 8], rb2r[:], op=ALU.add))
            lg_free[t % 2] = lgv_done[t]
            kb.op("vector", lambda: nc.vector.reduce_max(cs[:, 28:29], lgt[:], axis=AX.X))
            kb.op("vector", lambda: nc.vector.tensor_scalar(eqm[:], lgt[:], cs[:, 28:29], None, op0=ALU.is_ge))
            kb.op("vector", lambda: nc.vector.tensor_tensor(pr[:, 1:8], eqm[:, 1:8], eqm[:, 0:7], op=ALU.max))
            kb.op("vector", lambda: nc.vector.tensor_copy(pr[:, 0:1], eqm[:, 0:1]))
            kb.op("vector", lambda: nc.vector.tensor_tensor(lgt[:, 2:8], pr[:, 2:8], pr[:, 0:6], op=ALU.max))
            kb.op("vector", lambda: nc.vector.tensor_copy(lgt[:, 0:2], pr[:, 0:2]))
            kb.op("vector", lambda: nc.vector.tensor_tensor(pr[:, 4:8], lgt[:, 4:8], lgt[:, 0:4], op=ALU.max))
            kb.op("vector", lambda: nc.vector.tensor_copy(pr[:, 0:4], lgt[:, 0:4]))
            kb.op("vector", lambda: nc.vector.tensor_scalar(
                lgt[:, 0:7], pr[:, 0:7], -1.0, 1.0, op0=ALU.mult, op1=ALU.add))
            kb.op("vector", lambda t_=t: nc.vector.tensor_tensor(
                selm[:, t_ * 8 + 1: t_ * 8 + 8], eqm[:, 1:8], lgt[:, 0:7], op=ALU.mult))
            sel_done[t] = kb.op("vector", lambda t_=t: nc.vector.tensor_copy(
                selm[:, t_ * 8: t_ * 8 + 1], eqm[:, 0:1]))

        def emit_sel_transpose(t):
            kb.wt("tensor", SV, sel_done[t])
            if t < 2:
                kb.wt("tensor", SV, xe_done[NTP - 2])
                kb.wt("tensor", SS, xe_done_s[NTP - 2])
            else:
                kb.wt("tensor", SS, stv_done[t - 2])
            sbank = 6 + t % 2
            kb.op("tensor", lambda t_=t, b=sbank: nc.tensor.transpose(
                PS[0:8, b * 512: b * 512 + 128], selm[:, t_ * 8:(t_ + 1) * 8], ident[:]),
                inc=S_tp)
            kb.wt("scalar", S_tp, S_tp.n)
            stv_done[t] = kb.op("scalar", lambda t_=t, b=sbank: nc.scalar.copy(
                selT[0:8, t_ * 128:(t_ + 1) * 128], PS[0:8, b * 512: b * 512 + 128]))

        be_done = 0
        be_bank = [0] * 8
        ginit_done_ss = 0
        ginit_done_sv = 0
        gi_free = [0, 0, 0, 0]

        def emit_ginit_tile(t, Ae):
            nonlocal ginit_done_ss, ginit_done_sv
            p = t % 4
            bp = 2 * p
            if t < 4:
                kb.wt("tensor", SV, b2_done)
                if p < 2:
                    kb.wt("tensor", SS, Ae)
                elif p == 2:
                    kb.wt("tensor", SV, lgv_done[NTT - 1])
                else:
                    kb.wt("tensor", SS, stv_done[NTT - 1])
            else:
                kb.wt("tensor", SS if t % 2 == 0 else SV, gi_free[p])
            kb.wt("tensor", SS, stv_done[t])
            kb.op("tensor", lambda t_=t, b=bp: nc.tensor.matmul(
                PS[:, b * 512:(b + 1) * 512], selT[0:8, t_ * 128:(t_ + 1) * 128],
                b2bf[:, 0:512], start=True, stop=True))
            kb.op("tensor", lambda t_=t, b=bp: nc.tensor.matmul(
                PS[:, (b + 1) * 512:(b + 2) * 512], selT[0:8, t_ * 128:(t_ + 1) * 128],
                b2bf[:, 512:1024], start=True, stop=True), inc=S_mm)
            if t % 2 == 0:
                kb.wt("scalar", S_mm, S_mm.n)
                gi_free[p] = kb.op("scalar", lambda t_=t, b=bp: nc.scalar.activation(
                    g[:, t_ * 1024:(t_ + 1) * 1024], PS[:, b * 512: b * 512 + 1024],
                    AF.Identity))
                ginit_done_ss = gi_free[p]
            else:
                kb.wt("vector", S_mm, S_mm.n)
                gi_free[p] = kb.op("vector", lambda t_=t, b=bp: nc.vector.tensor_copy(
                    g[:, t_ * 1024:(t_ + 1) * 1024], PS[:, b * 512: b * 512 + 1024]))
                ginit_done_sv = gi_free[p]

        if n_experts == 0:
            for t in range(NTT):
                emit_logits_tile(t)
                if t >= 1:
                    emit_sel_transpose(t - 1)
            emit_sel_transpose(NTT - 1)
            for t in range(NTT):
                emit_ginit_tile(t, 0)

        for ex in range(n_experts):
            for blk in range(2):
                if ex == 0 and blk == 0:
                    bankA = bankR_free
                    acs = [c for c in range(16) if c not in A00_PRO]
                else:
                    bankA = [0, 0, 0, 0]
                    acs = list(range(16))
                ent_Be = be_done
                if ex == 0 and blk == 0:
                    kb.wt("tensor", SV, wv_game_done[1])
                n_log = 0
                for ci, c in enumerate(acs):
                    emit_A_chunk(ex, blk, c, bankA, ent_Be if ci == 0 else 0)
                    if ex == 0 and blk == 0:
                        tgt = 2 * (ci + 1) if ci < 4 else ci + 5
                        tgt = min(tgt, NTT)
                        while n_log < tgt:
                            if n_log >= 1:
                                emit_sel_transpose(n_log - 1)
                            emit_logits_tile(n_log)
                            n_log += 1
                if ex == 0 and blk == 0:
                    emit_sel_transpose(NTT - 1)
                    Ae_snap = SS.n
                    for t in range(NTT):
                        emit_ginit_tile(t, Ae_snap)
                Ae_total = SS.n
                for dh in range(2):
                    mm_base = S_mm.n
                    for c in range(8):
                        for ft in range(4):
                            if ft % 2 == 0:
                                gh = use_chunk(pidx[("B", ex, blk, dh, c, ft // 2)])
                            if c == 0 and ft == 0:
                                kb.wt("tensor", SS, Ae_total)
                                if ex == 0 and blk == 0 and dh == 0:
                                    kb.wt("tensor", SV, ginit_done_sv)
                            f = 4 * c + ft
                            for t in range(8):
                                if c == 0 and ft == 0:
                                    kb.wt("tensor", SV, be_bank[t])
                                stop = f == NFT - 1
                                half_last = ft % 2 == 1 and t == 7
                                inc = None
                                if half_last:
                                    inc = S_wuse
                                elif stop:
                                    inc = S_mm
                                kb.op("tensor", lambda F_=f, t_=t, i=gh % 4, ftl=ft:
                                      nc.tensor.matmul(PS[:, t_ * 512:(t_ + 1) * 512],
                                                       hidA[F_][:, t_ * 128:(t_ + 1) * 128],
                                                       whb[i][:, (ftl % 2) * 512:(ftl % 2) * 512 + 512],
                                                       start=(F_ == 0), stop=(F_ == NFT - 1)),
                                      inc=inc)
                    assert S_mm.n == mm_base + 7
                    gc_last = pidx[("B", ex, blk, dh, 7, 1)]
                    if (ex, blk, dh) == (0, 0, 0):
                        kb.wt("vector", SS, ginit_done_ss)
                    for t in range(8):
                        if t == 7:
                            kb.wt("vector", S_wuse, wuse_after[gc_last])
                        else:
                            kb.wt("vector", S_mm, mm_base + t + 1)
                        tg = blk * 8 + t
                        be_done = kb.op("vector", lambda t_=t, tg_=tg, d=dh, E=ex:
                                        nc.vector.scalar_tensor_tensor(
                                            g[:, tg_ * 1024 + d * 512: tg_ * 1024 + d * 512 + 512],
                                            PS[:, t_ * 512:(t_ + 1) * 512],
                                            selm[:, tg_ * 8 + E: tg_ * 8 + E + 1],
                                            g[:, tg_ * 1024 + d * 512: tg_ * 1024 + d * 512 + 512],
                                            op0=ALU.mult, op1=ALU.add))
                        be_bank[t] = be_done

        if stage == "gexp":
            _emit_dump(kb, nc, (SV, SS, S_out),
                       [g.ap()[:, i * 1024:(i + 1) * 1024] for i in range(16)], ostage, OUT)
            kb.wt("sync", S_out, S_out.n)
            return nc, kb

        ensure(len(plan))
        wbt_done = max((v for s, v in cast_done[n_premm:] if s is SS), default=0)
        wbt_sv = max((v for s, v in cast_done[n_premm:] if s is SV), default=0)
        assert S_wuse.n == total_wuse
        kb.wt("scalar", S_wuse, total_wuse)
        kb.wt("scalar", SS, wbt_done)
        kb.wt("scalar", SV, wbt_sv)
        kb.dma("scalar", lambda: nc.scalar.dma_start(out=w1r, in_=W1R), S_sc)
        kb.dma("scalar", lambda: nc.scalar.dma_start(out=scrC, in_=W2R), S_sc)
        kb.dma("scalar", lambda: nc.scalar.dma_start(out=brep[:], in_=BREP), S_sc)
        pre_final_sc = S_sc.n
        kb.wt("vector", S_sc, pre_final_sc)
        kb.wt("vector", SS, wbt_done)
        kb.wt("vector", S_wuse, total_wuse)
        w2rb_done = kb.op("vector", lambda: nc.vector.tensor_copy(w2rb[:], scrC))

        fv_free = [0, 0]
        s1_done = {}
        y_done = {}
        o_done = {}
        acc_done = {}
        add_done = {}
        for tt in range(NTT):
            cb = (tt % 2) * 16
            if tt <= 1:
                kb.wt("sync", SV, xn_done[NTP - 1])
                kb.wt("sync", SS, acc_pro[NTP - 1 - tt])
            if tt >= 2:
                kb.wt("sync", SG, add_done[tt - 2])
                kb.wt("sync", SS, acc_done[tt - 2])
            u_ms[NTP + tt] = kb.dma("sync", lambda t=tt: nc.sync.dma_start(
                out=ustage[t % 2][:], in_=U[t * 128:(t + 1) * 128, :]), S_u[tt % 2])
            if tt >= 1:
                kb.wt("sync", SG, add_done[tt - 1])
                kb.dma("sync", lambda t=tt - 1: nc.sync.dma_start(
                    out=OUT[:, t * 1024:(t + 1) * 1024], in_=ostage[:]), S_out)
            kb.wt("scalar", S_u[tt % 2], u_ms[NTP + tt])
            acc_done[tt] = ln_acc(tt, ustage[tt % 2], cb)
            xfb = (tt % 2) * 4
            kb.wt("tensor", SV, fv_free[tt % 2])
            if tt == 0:
                kb.wt("tensor", SV, max(be_done, wbt_sv))
                kb.wt("tensor", SS, wbt_done)
            for which, woff, shift in ((0, 16, 0), (1, 24, 1)):
                for ec in range(2):
                    bank = xfb + which * 2 + ec
                    for dt in range(ND):
                        kb.op("tensor", lambda d=dt, b=bank, w=woff, E=ec, t=tt, sh=shift:
                              nc.tensor.matmul(PS[:, b * 512:(b + 1) * 512],
                                               xnT[d][:, t * 128 + sh: t * 128 + sh + 128],
                                               hidA[w + d][:, E * 512:(E + 1) * 512],
                                               start=(d == 0), stop=(d == ND - 1)),
                              inc=(S_mm if (which, ec, dt) == (1, 1, ND - 1) else None))
            PS_xf = PS[:, xfb * 512: xfb * 512 + 1024]
            PS_xb = PS[:, (xfb + 2) * 512: (xfb + 2) * 512 + 1024]
            P = scrC if tt % 2 == 0 else scrB
            kb.wt("scalar", S_mm, S_mm.n)
            if tt == 0:
                kb.wt("scalar", SV, w2rb_done)
            if tt == 1:
                kb.wt("scalar", SV, y_done[0])
            if tt >= 2:
                kb.wt("scalar", SG, o_done[tt - 2])
                kb.wt("scalar", SV, y_done[tt - 2])
            kb.op("scalar", lambda p=PS_xb, o=P: nc.scalar.activation(o, p, AF.Identity))
            xbcp = SS.n
            kb.wt("vector", S_mm, S_mm.n)
            if tt == 0:
                kb.wt("vector", SS, wbt_done)
                kb.wt("vector", S_wuse, total_wuse)
                kb.wt("vector", S_sc, pre_final_sc)
            interior = 0 < tt < NTT - 1
            if not interior:
                xn_fin = ln_vec(tt, ustage[tt % 2], scrA, acc_done[tt], cb)
            if tt == 0:
                kb.op("vector", lambda p=PS_xf: nc.vector.tensor_tensor(junkF, scrA, p, op=ALU.subtract))
                kb.op("vector", lambda p=PS_xf, t=tt: nc.vector.scalar_tensor_tensor(
                    junkF, junkF, m0[:, t:t + 1], p, op0=ALU.mult, op1=ALU.add))
                kb.op("vector", lambda: nc.vector.tensor_copy(scrB, junkF))
                xf_ap = scrB
            else:
                xf_ap = PS_xf
            if interior:
                gcol = (xfb + 2) * 512
                kb.wt("tensor", SS, xbcp)
                for which in range(2):
                    for dt in range(ND):
                        kb.op("tensor", lambda d=dt, w=which, t=tt, gc_=gcol:
                              nc.tensor.matmul(PS[:, gc_: gc_ + 1],
                                               xnT[d][:, t * 128 + w: t * 128 + w + 128],
                                               wvs[:, w * 8 + d: w * 8 + d + 1],
                                               start=(w == 0 and d == 0),
                                               stop=(w == 1 and d == ND - 1)),
                              inc=(S_mm if (which, dt) == (1, ND - 1) else None))
                gm_done = S_mm.n
                xn_fin = ln_vec(tt, ustage[tt % 2], scrA, acc_done[tt], cb)
                kb.wt("scalar", S_mm, gm_done)
                fsig = kb.op("scalar", lambda gc_=gcol, c=cb: nc.scalar.activation(
                    cs[:, c + 10:c + 11], PS[:, gc_: gc_ + 1], AF.Sigmoid, bias=brep[:]))
            else:
                kb.op("vector", lambda x=xf_ap: nc.vector.tensor_tensor(junkF, x, w1r, op=ALU.mult))
                kb.op("vector", lambda c=cb: nc.vector.reduce_sum(cs[:, c + 8:c + 9], junkF, axis=AX.X))
                kb.wt("vector", SS, xbcp)
                if tt == NTT - 1:
                    kb.op("vector", lambda o=P: nc.vector.tensor_tensor(junkF, scrA, o, op=ALU.subtract))
                    kb.op("vector", lambda o=P, t=tt: nc.vector.scalar_tensor_tensor(
                        o, junkF, mL[:, t:t + 1], o, op0=ALU.mult, op1=ALU.add))
                kb.op("vector", lambda o=P: nc.vector.tensor_tensor(junkF, o, w2rb[:], op=ALU.mult))
                kb.op("vector", lambda c=cb: nc.vector.reduce_sum(cs[:, c + 9:c + 10], junkF, axis=AX.X))
                fg = kb.op("vector", lambda c=cb: nc.vector.tensor_tensor(
                    cs[:, c + 10:c + 11], cs[:, c + 8:c + 9], cs[:, c + 9:c + 10], op=ALU.add))
                kb.wt("scalar", SV, fg)
                fsig = kb.op("scalar", lambda c=cb: nc.scalar.activation(
                    cs[:, c + 10:c + 11], cs[:, c + 10:c + 11], AF.Sigmoid, bias=brep[:]))
            kb.wt("scalar", S_out, 16 * tt)
            if tt >= 1:
                kb.wt("scalar", SG, add_done[tt - 1])
            fsil = kb.op("scalar", lambda t=tt: nc.scalar.activation(
                ostage[:], g[:, t * 1024:(t + 1) * 1024], AF.Sigmoid))
            kb.wt("vector", SS, fsig)
            s1_done[tt] = kb.op("vector", lambda x=xf_ap, o=P: nc.vector.tensor_tensor(o, x, o, op=ALU.add))
            fv_free[tt % 2] = s1_done[tt]
            kb.op("vector", lambda o=P: nc.vector.tensor_tensor(o, o, scrA, op=ALU.subtract))
            y_done[tt] = kb.op("vector", lambda c=cb, o=P: nc.vector.scalar_tensor_tensor(
                o, o, cs[:, c + 10:c + 11], scrA, op0=ALU.mult, op1=ALU.add))
            kb.wt("gpsimd", SS, fsil)
            kb.wt("gpsimd", SV, y_done[tt])
            kb.wt("gpsimd", S_u[tt % 2], u_ms[NTP + tt])
            kb.op("gpsimd", lambda t=tt: nc.gpsimd.tensor_tensor(
                ostage[:], ostage[:], g[:, t * 1024:(t + 1) * 1024], op=ALU.mult))
            o_done[tt] = kb.op("gpsimd", lambda o=P: nc.gpsimd.tensor_tensor(
                ostage[:], o, ostage[:], op=ALU.mult))
            add_done[tt] = kb.op("gpsimd", lambda t=tt: nc.gpsimd.tensor_tensor(
                ostage[:], ostage[:], ustage[t % 2][:], op=ALU.add))

        kb.wt("sync", SG, add_done[NTT - 1])
        kb.dma("sync", lambda t=NTT - 1: nc.sync.dma_start(
            out=OUT[:, t * 1024:(t + 1) * 1024], in_=ostage[:]), S_out)
        kb.wt("sync", S_out, S_out.n)
        return nc, kb


def _emit_dump(kb, nc, sems, aps, ostage, OUT):
    SV, SS, S_out = sems
    for i, ap in enumerate(aps[:16]):
        if i >= 1:
            kb.wt("vector", S_out, 16 * i)
        w = ap.shape[-1]
        if w < 1024:
            kb.op("vector", lambda: nc.vector.memset(ostage[:], 0.0))
        cp = kb.op("vector", lambda a=ap, W=w: nc.vector.tensor_copy(ostage[:, 0:W], a))
        kb.wt("scalar", SV, cp)
        kb.dma("scalar", lambda j=i: nc.scalar.dma_start(
            out=OUT[:, j * 1024:(j + 1) * 1024], in_=ostage[:]), S_out)


def _run(nc, kb):
    with nc.Block() as block:
        @block.sync
        def _(eng):
            kb.run("sync")

        @block.scalar
        def _(eng):
            kb.run("scalar")

        @block.vector
        def _(eng):
            kb.run("vector")

        @block.tensor
        def _(eng):
            kb.run("tensor")

        @block.gpsimd
        def _(eng):
            kb.run("gpsimd")


def build_nc(stage="full", n_experts=NE):
    nc, kb = _build(stage, n_experts)
    _run(nc, kb)
    return nc


def host_inputs(inputs):
    f32 = np.float32
    u = np.ascontiguousarray(np.asarray(inputs["L0_attn_u"], f32))
    W1 = np.asarray(inputs["L0_ffn_expert_w1"], f32)
    W2 = np.asarray(inputs["L0_ffn_expert_w2"], f32)
    rw1 = np.asarray(inputs["L0_ffn_router_w1"], f32)
    rw2 = np.asarray(inputs["L0_ffn_router_w2"], f32)
    rb1 = np.asarray(inputs["L0_ffn_router_b1"], f32)
    rb2 = np.asarray(inputs["L0_ffn_router_b2"], f32)
    b1 = np.asarray(inputs["L0_ffn_expert_b1"], f32)
    b2 = np.asarray(inputs["L0_ffn_expert_b2"], f32)
    WB = np.asarray(inputs["L0_attn_WB"], f32)
    WBf = np.asarray(inputs["L0_attn_WBf"], f32)
    w1 = np.asarray(inputs["L0_attn_w1"], f32)
    w2 = np.asarray(inputs["L0_attn_w2"], f32)
    b = np.asarray(inputs["L0_attn_b"], f32)

    shared = {
        "W1L": np.ascontiguousarray(
            W1.reshape(8, 8, 128, 16, 2, 128).transpose(0, 3, 2, 4, 1, 5).reshape(8, 16, 128, 2048)),
        "W2L": np.ascontiguousarray(
            W2.reshape(8, 8, 4, 128, 2, 512).transpose(0, 4, 1, 3, 2, 5).reshape(8, 2, 8, 128, 2048)),
        "RW1L": np.ascontiguousarray(
            rw1.reshape(8, 128, 4, 2, 128).transpose(2, 1, 3, 0, 4).reshape(4, 128, 2048)),
        "WBTL": np.ascontiguousarray(
            WB.T.reshape(4, 2, 128, 1024).transpose(0, 2, 1, 3).reshape(4, 128, 2048)),
        "WBFTL": np.ascontiguousarray(
            WBf.T.reshape(4, 2, 128, 1024).transpose(0, 2, 1, 3).reshape(4, 128, 2048)),
        "RW2L": np.ascontiguousarray(rw2.reshape(8, 128, 8).transpose(1, 0, 2).reshape(128, 64)),
        "B1L": np.ascontiguousarray(b1.reshape(8, 32, 128).transpose(2, 0, 1).reshape(128, 256)),
        "B2L": np.ascontiguousarray(b2),
        "RB1L": np.ascontiguousarray(rb1.reshape(8, 128).T),
        "RB2R": np.ascontiguousarray(np.tile(rb2[None, :], (128, 1))),
        "W1R": np.ascontiguousarray(np.tile(w1.reshape(1, 1024), (128, 1))),
        "W2R": np.ascontiguousarray(np.tile(w2.reshape(1, 1024), (128, 1))),
        "BREP": np.full((128, 1), b.reshape(-1)[0], f32),
        "IDENT": np.eye(128, dtype=f32),
        "WBL": np.ascontiguousarray(WB.reshape(8, 128, 1024)),
        "WBFL": np.ascontiguousarray(WBf.reshape(8, 128, 1024)),
        "W1C": np.ascontiguousarray(w1.reshape(8, 128).T),
        "W2C": np.ascontiguousarray(w2.reshape(8, 128).T),
    }
    in_maps = []
    for c in range(8):
        bb, h = c // 2, c % 2
        if h == 0:
            shard = u[bb, 0:2049]
        else:
            shard = np.concatenate([u[bb, 2048:4096], np.zeros((1, 1024), f32)], 0)
        shard = np.concatenate([shard, np.zeros((2176 - 2049, 1024), f32)], 0)
        msk0 = np.zeros(2048, f32)
        mskL = np.zeros(2048, f32)
        if h == 0:
            msk0[0] = 1.0
        else:
            mskL[2047] = 1.0
        m = dict(shared)
        m["U"] = np.ascontiguousarray(shard)
        m["M0"] = np.ascontiguousarray(msk0.reshape(16, 128).T)
        m["ML"] = np.ascontiguousarray(mskL.reshape(16, 128).T)
        in_maps.append(m)
    return in_maps


def assemble(results):
    outs = []
    for c in range(8):
        o = results[c]["OUT"].reshape(128, 16, 1024).transpose(1, 0, 2).reshape(2048, 1024)
        outs.append(o)
    return np.stack([np.concatenate([outs[2 * bb], outs[2 * bb + 1]], 0)
                     for bb in range(4)], 0).astype(np.float32)


def kernel(**inputs):
    nc = build_nc("full", NE)
    in_maps = host_inputs(inputs)
    res = run_bass_kernel_spmd(nc, in_maps, core_ids=list(range(8)))
    return assemble(res.results)
